# version 02
# baseline (speedup 1.0000x reference)
import numpy as np
from contextlib import ExitStack

import concourse.bass as bass
import concourse.mybir as mybir
from concourse.bass_utils import run_bass_kernel_spmd
from concourse.replica_groups import maybe_share_collective_output_space

F32 = mybir.dt.float32
I32 = mybir.dt.int32
AF = mybir.ActivationFunctionType
ALU = mybir.AluOpType
AX = mybir.AxisListType

NCORES = 8
D = 1024
T = 4096
DC = D // 128
LSEG = 512
NSEG = T // LSEG
H = 16
DK = 64
HID = 3072
HC = HID // 128
RG8 = [[0, 1, 2, 3, 4, 5, 6, 7]]
C127 = float(np.float32(1.0 / 127.0))

ENGS = ("PE", "ACT", "DVE", "POOL", "SP")
_BA = {"PE": "tensor", "ACT": "scalar", "DVE": "vector", "POOL": "gpsimd", "SP": "sync"}

DEBUG = False
_DBG_SHAPES = {}
BUILD_STAGE = 2
ATT_SUB = 7


class Sched:

    def __init__(self, nc, block, ctx):
        self.nc, self.block, self.ctx = nc, block, ctx
        self.psem = {e: ctx.enter_context(nc.semaphore(f"P_{e}")) for e in ENGS}
        self.pcnt = {e: 0 for e in ENGS}
        self.drain_at = {e: 0 for e in ENGS}
        self.synced = {}
        self.dma_synced = {}
        self.writer = {}
        self.readers = {}
        self.dma_sem = {}

    def _slot(self, slot):
        if slot not in self.dma_sem:
            sem = self.ctx.enter_context(self.nc.semaphore(f"dma_{slot}"))
            self.dma_sem[slot] = (sem, [0])
        return self.dma_sem[slot]

    def _wait(self, eng, h, tok):
        if tok[0] == "E":
            _, peng, cnt = tok
            if peng == eng:
                if self.drain_at[eng] < cnt:
                    h.drain()
                    self.drain_at[eng] = self.pcnt[eng]
            else:
                key = (eng, peng)
                if self.synced.get(key, 0) < cnt:
                    h.wait_ge(self.psem[peng], cnt)
                    self.synced[key] = cnt
        else:
            _, slot, cnt16 = tok
            sem, _ = self._slot(slot)
            key = (eng, slot)
            if self.dma_synced.get(key, 0) < cnt16:
                h.wait_ge(sem, cnt16)
                self.dma_synced[key] = cnt16

    def _deps(self, reads, writes):
        toks = []
        for b in reads:
            w = self.writer.get(b)
            if w is not None:
                toks.append(w)
        for b in writes:
            w = self.writer.get(b)
            if w is not None:
                toks.append(w)
            toks.extend(self.readers.get(b, ()))
        best = {}
        for t in toks:
            k = t[:2]
            if k not in best or best[k][2] < t[2]:
                best[k] = t
        return list(best.values())

    def _commit(self, tok, reads, writes):
        for b in writes:
            self.writer[b] = tok
            self.readers[b] = []
        for b in reads:
            self.readers.setdefault(b, []).append(tok)

    def run(self, eng, emit, reads=(), writes=()):
        deps = self._deps(reads, writes)
        sem = self.psem[eng]

        def body(h):
            for t in deps:
                self._wait(eng, h, t)
            last = emit(h)
            assert last is not None, "emit() must return its last instruction"
            last.then_inc(sem, 1)

        getattr(self.block, _BA[eng])(body)
        self.pcnt[eng] += 1
        tok = ("E", eng, self.pcnt[eng])
        self._commit(tok, reads, writes)
        return tok

    def dma(self, eng, emit, reads=(), writes=(), slot=None, serialize=True):
        assert slot is not None
        deps = self._deps(reads, writes)
        sem, cnt = self._slot(slot)
        cnt[0] += 16
        target = cnt[0]
        if serialize and target > 16:
            deps = deps + [("D", slot, target - 16)]

        def body(h):
            for t in deps:
                self._wait(eng, h, t)
            emit(h).then_inc(sem, 16)

        getattr(self.block, _BA[eng])(body)
        tok = ("D", slot, target)
        self._commit(tok, reads, writes)
        return tok

    def wait_tokens(self, eng, toks):
        def body(h):
            for t in toks:
                self._wait(eng, h, t)
        getattr(self.block, _BA[eng])(body)



def _l0_mask():
    t = np.arange(512)
    gp, rem = t // 256, t % 256
    g2, r = rem // 32, rem % 32
    lh = 16 * r + 2 * g2 + gp
    return (lh[None, :] >= lh[:, None]).astype(np.float32)


def _prep_inputs(inputs):
    x = np.ascontiguousarray(np.asarray(inputs["L0_attn_x"], dtype=np.float32))
    ident = np.eye(128, dtype=np.float32)
    ones = np.ones((128, 128), dtype=np.float32)
    dup64 = np.zeros((64, 128), dtype=np.float32)
    dup64[np.arange(64), np.arange(64)] = 1.0
    dup64[np.arange(64), 64 + np.arange(64)] = 1.0
    swap = np.zeros((128, 128), dtype=np.float32)
    swap[np.arange(128), np.arange(128) ^ 64] = 1.0
    dupswap = np.zeros((128, 128), dtype=np.float32)
    dupswap[np.arange(128), np.arange(128)] = 1.0
    dupswap[np.arange(128), np.arange(128) ^ 64] = 1.0
    M0 = np.asarray(inputs["L0_attn_memory0"], np.float32).reshape(H, DK, DK)
    Mdup = np.zeros((128, H * 64), dtype=np.float32)
    for h in range(H):
        Mdup[0:64, 64 * h:64 * h + 64] = M0[h]
        Mdup[64:128, 64 * h:64 * h + 64] = M0[h]
    Z0 = np.asarray(inputs["L0_attn_z0"], np.float32).reshape(H, DK)
    Zdup = np.zeros((128, H), dtype=np.float32)
    for h in range(H):
        Zdup[0:64, h] = Z0[h]
        Zdup[64:128, h] = Z0[h]
    betas = np.asarray(inputs["L0_attn_betas"], np.float32).reshape(H, DK)
    base = {
        "ident": ident, "ones": ones, "dup64": dup64, "swapm": swap, "dupswap": dupswap,
        "mask01": np.ascontiguousarray(_l0_mask()),
        "betasc": np.ascontiguousarray(betas.T.copy()),
        "Mdup0": np.ascontiguousarray(Mdup),
        "Zdup0": np.ascontiguousarray(Zdup),
        "Wq0": np.ascontiguousarray(np.asarray(inputs["L0_attn_Wq"], np.float32)),
        "Wk0": np.ascontiguousarray(np.asarray(inputs["L0_attn_Wk"], np.float32)),
        "Wv0": np.ascontiguousarray(np.asarray(inputs["L0_attn_Wv"], np.float32)),
        "Wo0": np.ascontiguousarray(np.asarray(inputs["L0_attn_Wo"], np.float32)),
        "wgT0": np.ascontiguousarray(np.asarray(inputs["L0_ffn_w_gate"], np.int32).T),
        "wuT0": np.ascontiguousarray(np.asarray(inputs["L0_ffn_w_up"], np.int32).T),
        "wdT0": np.ascontiguousarray(np.asarray(inputs["L0_ffn_w_down"], np.int32).T),
        "Wq1": np.ascontiguousarray(np.asarray(inputs["L1_attn_Wq"], np.float32)),
        "Wk1": np.ascontiguousarray(np.asarray(inputs["L1_attn_Wk"], np.float32)),
        "Wv1": np.ascontiguousarray(np.asarray(inputs["L1_attn_Wv"], np.float32)),
        "Wo1": np.ascontiguousarray(np.asarray(inputs["L1_attn_Wo"], np.float32)),
        "bo1": np.ascontiguousarray(np.asarray(inputs["L1_attn_bo"], np.float32).reshape(1, D)),
        "wg1": np.ascontiguousarray(np.asarray(inputs["L1_ffn_w_gate"], np.float32)),
        "wu1": np.ascontiguousarray(np.asarray(inputs["L1_ffn_w_up"], np.float32)),
        "wd1": np.ascontiguousarray(np.asarray(inputs["L1_ffn_w_down"], np.float32)),
    }
    maps = []
    for b in range(NCORES):
        m = dict(base)
        m["xT"] = np.ascontiguousarray(x[b].T)
        maps.append(m)
    return maps


class Ctx:
    pass


def _hbm(nc, C):
    C.ap = {}

    def din(name, shape, dtype=F32):
        C.ap[name] = nc.dram_tensor(name, list(shape), dtype, kind="ExternalInput").ap()

    def dsc(name, shape, dtype=F32):
        C.ap[name] = nc.dram_tensor(name, list(shape), dtype, kind="Internal").ap()

    din("xT", (D, T))
    for nm, sh in (("ident", (128, 128)), ("ones", (128, 128)), ("dup64", (64, 128)),
                   ("swapm", (128, 128)), ("dupswap", (128, 128)), ("mask01", (512, 512)),
                   ("betasc", (64, H)), ("Mdup0", (128, H * 64)), ("Zdup0", (128, H)),
                   ("bo1", (1, D))):
        din(nm, sh)
    for w in ("Wq0", "Wk0", "Wv0", "Wo0", "Wq1", "Wk1", "Wv1", "Wo1"):
        din(w, (D, D))
    din("wgT0", (D, HID), I32)
    din("wuT0", (D, HID), I32)
    din("wdT0", (HID, D), I32)
    din("wg1", (HID, D))
    din("wu1", (HID, D))
    din("wd1", (D, HID))
    C.out_fm = nc.dram_tensor("out_fm", [D, T], F32, kind="ExternalOutput").ap()
    for nm, sh in (("wgT0_f", (D, HID)), ("wuT0_f", (D, HID)), ("wdT0_f", (HID, D)),
                   ("g_sp", (D, T)), ("x1_sp", (D, T)), ("xn1_sp", (D, T)), ("att_sp", (D, T)),
                   ("up_sp", (HID, T)), ("gate_sp", (HID, T)), ("t_sp", (HID, T)),
                   ("upq_sp", (HID, T)), ("u_sp", (HID, T)),
                   ("wgT1_f", (D, HID)), ("wuT1_f", (D, HID)), ("wdT1_f", (HID, D))):
        dsc(nm, sh)
    space = maybe_share_collective_output_space("AllReduce", RG8)
    dsc("cc_in", (1, 8))
    C.ap["cc_out"] = nc.dram_tensor("cc_out", [1, 8], F32, kind="Internal", addr_space=space).ap()
    C.dbg = {}
    if DEBUG:
        for name, sh in _DBG_SHAPES.items():
            C.dbg[name] = nc.dram_tensor("dbg_" + name, list(sh), F32, kind="ExternalOutput").ap()


def _dbg(S, C, name, src_ap, reads):
    if DEBUG and name in C.dbg:
        S.dma("POOL", lambda h: h.dma_start(out=C.dbg[name][:, :], in_=src_ap),
              reads=list(reads), writes=[], slot="dbgw")


def _acc(S, bank_ap, pairs, reads, writes):
    def emit(h):
        last = None
        n = len(pairs)
        for i, (lhsT, rhs) in enumerate(pairs):
            last = h.matmul(bank_ap, lhsT, rhs, start=(i == 0), stop=(i == n - 1))
        return last
    return S.run("PE", emit, reads=reads, writes=writes)


def build_program():
    nc = bass.Bass("TRN2", target_bir_lowering=False)
    C = Ctx()
    _hbm(nc, C)

    with ExitStack() as ctx:
        E = ctx.enter_context
        C.consts = E(nc.sbuf_tensor("consts", [128, 640], F32))
        C.mask = E(nc.sbuf_tensor("mask", [128, 4 * 512], F32))
        C.gates = E(nc.sbuf_tensor("gates", [64, 3 * H], F32))
        C.Zd = E(nc.sbuf_tensor("Zd", [128, H], F32))
        C.Md = E(nc.sbuf_tensor("Md", [128, H * 64], F32))
        C.row = E(nc.sbuf_tensor("row", [1, 2048], F32))
        C.srow = E(nc.sbuf_tensor("srow", [128, 16], F32))
        C.cb = E(nc.sbuf_tensor("cb", [128, 8], F32))
        C.xtile = E(nc.sbuf_tensor("xtile", [128, DC * 512], F32))
        C.xn = E(nc.sbuf_tensor("xn", [128, DC * 512], F32))
        C.stg = E(nc.sbuf_tensor("stg", [128, DC * 512], F32))
        C.qg = E(nc.sbuf_tensor("qg", [128, H * 256], F32))
        C.kg = E(nc.sbuf_tensor("kg", [128, H * 256], F32))
        C.vg = E(nc.sbuf_tensor("vg", [128, H * 256], F32))
        C.qsw = E(nc.sbuf_tensor("qsw", [128, H * 256], F32))
        C.upool = E(nc.sbuf_tensor("upool", [128, 4 * 512], F32))
        C.tokp = E(nc.sbuf_tensor("tokp", [128, 8 * 128], F32))
        C.scr = E(nc.sbuf_tensor("scr", [128, 4 * 512], F32))
        C.pw = [E(nc.sbuf_tensor(f"pw{i}", [128, DC * 128], F32)) for i in range(2)]
        C.fw = [E(nc.sbuf_tensor(f"fw{i}", [128, HID], F32)) for i in range(2)]
        C.fwi = E(nc.sbuf_tensor("fwi", [128, HID], I32))
        C.hdb = E(nc.sbuf_tensor("hdb", [128, 2 * 512], F32))
        C.ysb = E(nc.sbuf_tensor("ysb", [128, 2 * 512], F32))
        C.ps = [E(nc.psum_tensor(f"ps{i}", [128, 512], F32)) for i in range(8)]

        block = E(nc.Block())
        S = Sched(nc, block, ctx)

        C.IDENT = C.consts[:, 0:128]
        C.ONES = C.consts[:, 128:256]
        C.DUP64 = C.consts[0:64, 256:384]
        C.SWAP = C.consts[:, 384:512]
        C.DUPSW = C.consts[:, 512:640]
        C.scrA = C.scr[:, 0:512]
        C.scrB = C.scr[:, 512:1024]
        C.scrC = C.scr[:, 1024:1536]
        C.scrD = C.scr[:, 1536:2048]

        _init(nc, S, C)
        _l0_weight_convert(nc, S, C)
        _phase_a(nc, S, C)

        for dc in range(DC):
            for tt in range(NSEG):
                S.dma("POOL", lambda h, dc=dc, tt=tt: h.dma_start(
                    out=C.out_fm[128 * dc:128 * dc + 128, 512 * tt:512 * tt + 512],
                    in_=C.ap["x1_sp"][128 * dc:128 * dc + 128, 512 * tt:512 * tt + 512]),
                    reads=[f"x1sp_{dc}_{tt}"], writes=[], slot="outw", serialize=False)
        sem, cnt = S._slot("outw")
        S.wait_tokens("POOL", [("D", "outw", cnt[0])])

    return nc


def _init(nc, S, C):
    ap = C.ap
    S.dma("SP", lambda h: h.dma_start(out=C.consts[:, 0:128], in_=ap["ident"][:, :]), writes=["ident"], slot="init")
    S.dma("SP", lambda h: h.dma_start(out=C.consts[:, 128:256], in_=ap["ones"][:, :]), writes=["ones"], slot="init")
    S.dma("SP", lambda h: h.dma_start(out=C.consts[0:64, 256:384], in_=ap["dup64"][:, :]), writes=["dup64"], slot="init")
    S.dma("SP", lambda h: h.dma_start(out=C.consts[:, 384:512], in_=ap["swapm"][:, :]), writes=["swapm"], slot="init")
    S.dma("SP", lambda h: h.dma_start(out=C.consts[:, 512:640], in_=ap["dupswap"][:, :]), writes=["dupsw"], slot="init")
    for jc in range(4):
        S.dma("SP", lambda h, jc=jc: h.dma_start(
            out=C.mask[:, 512 * jc:512 * jc + 512], in_=ap["mask01"][128 * jc:128 * jc + 128, :]),
            writes=["mask"], slot="init")
    S.dma("SP", lambda h: h.dma_start(out=C.gates[:, 0:H], in_=ap["betasc"][:, :]), writes=["betas"], slot="init")
    S.dma("SP", lambda h: h.dma_start(out=C.Zd[:, :], in_=ap["Zdup0"][:, :]), writes=["Zd"], slot="init")
    S.dma("SP", lambda h: h.dma_start(out=C.Md[:, :], in_=ap["Mdup0"][:, :]), writes=["Md"], slot="init")
    S.run("DVE", lambda h: h.memset(C.cb[:, 2:3], 1e-5), reads=[], writes=["cb_eps"])
    S.run("ACT", lambda h: h.activation(out=C.gates[:, H:2 * H], in_=C.gates[:, 0:H], func=AF.Sigmoid),
          reads=["betas"], writes=["gate"])
    S.run("DVE", lambda h: h.tensor_scalar(C.gates[:, 2 * H:3 * H], C.gates[:, H:2 * H], -1.0, 1.0, ALU.mult, ALU.add),
          reads=["gate"], writes=["gate1m"])


def _l0_weight_convert(nc, S, C):
    for src, dst, nchunk, width in (("wgT0", "wgT0_f", DC, HID),
                                    ("wuT0", "wuT0_f", DC, HID),
                                    ("wdT0", "wdT0_f", HC, D)):
        for c in range(nchunk):
            ib = c % 2
            S.dma("SP", lambda h, src=src, c=c, width=width: h.dma_start(
                out=C.fwi[:, 0:width], in_=C.ap[src][128 * c:128 * c + 128, :]),
                reads=[], writes=["fwi"], slot="cvt_i")
            S.run("DVE", lambda h, width=width, ib=ib: h.tensor_copy(C.fw[ib][:, 0:width], C.fwi[:, 0:width]),
                  reads=["fwi"], writes=[f"fw{ib}"])
            S.dma("POOL", lambda h, dst=dst, c=c, width=width, ib=ib: h.dma_start(
                out=C.ap[dst][128 * c:128 * c + 128, :], in_=C.fw[ib][:, 0:width]),
                reads=[f"fw{ib}"], writes=[f"{dst}_c{c}"], slot=f"cvto{ib}")


XN = [f"xn_{i}" for i in range(DC)]
XT = [f"xt_{i}" for i in range(DC)]
STG = [f"stg_{i}" for i in range(DC)]


def _load_xtile(S, C, tt):
    for dc in range(DC):
        S.dma("SP", lambda h, tt=tt, dc=dc: h.dma_start(
            out=C.xtile[:, 512 * dc:512 * dc + 512],
            in_=C.ap["xT"][128 * dc:128 * dc + 128, 512 * tt:512 * tt + 512]),
            writes=[f"xt_{dc}"], slot="xin")


def _ln_tile(S, C):
    ONES = C.ONES
    _acc(S, C.ps[0][0:1, :],
         [(ONES[:, 0:1], C.xtile[:, 512 * dc:512 * dc + 512]) for dc in range(DC)],
         reads=XT + ["ones"], writes=["ps0"])
    S.run("ACT", lambda h: h.activation(out=C.xn[:, :], in_=C.xtile[:, :], func=AF.Square),
          reads=XT, writes=XN)
    _acc(S, C.ps[1][0:1, :],
         [(ONES[:, 0:1], C.xn[:, 512 * dc:512 * dc + 512]) for dc in range(DC)],
         reads=XN + ["ones"], writes=["ps1"])
    mrow = C.row[0:1, 0:512]
    rrow = C.row[0:1, 512:1024]
    S.run("DVE", lambda h: h.tensor_scalar(mrow, C.ps[0][0:1, :], 1.0 / 1024.0, None, ALU.mult, ALU.bypass),
          reads=["ps0"], writes=["rowM"])
    S.run("DVE", lambda h: h.tensor_scalar(rrow, C.ps[1][0:1, :], 1.0 / 1024.0, None, ALU.mult, ALU.bypass),
          reads=["ps1"], writes=["rowR"])
    S.run("DVE", lambda h: h.tensor_tensor(C.row[0:1, 1024:1536], mrow, mrow, ALU.mult),
          reads=["rowM"], writes=["rowC"])
    S.run("DVE", lambda h: h.tensor_tensor(rrow, rrow, C.row[0:1, 1024:1536], ALU.subtract),
          reads=["rowR", "rowC"], writes=["rowR"])
    S.run("ACT", lambda h: h.activation(out=rrow, in_=rrow, func=AF.Sqrt, bias=C.cb[0:1, 2:3]),
          reads=["rowR", "cb_eps"], writes=["rowR"])
    S.run("DVE", lambda h: h.reciprocal(rrow, rrow), reads=["rowR"], writes=["rowR"])
    S.run("PE", lambda h: h.matmul(C.ps[0][:, :], ONES[0:1, 0:128], mrow, start=True, stop=True),
          reads=["rowM", "ones", "ps0"], writes=["ps0"])
    S.run("PE", lambda h: h.matmul(C.ps[1][:, :], ONES[0:1, 0:128], rrow, start=True, stop=True),
          reads=["rowR", "ones", "ps1"], writes=["ps1"])
    S.run("ACT", lambda h: h.activation(out=C.scrA, in_=C.ps[0][:, :], func=AF.Copy),
          reads=["ps0"], writes=["scrA"])
    S.run("ACT", lambda h: h.activation(out=C.scrB, in_=C.ps[1][:, :], func=AF.Copy),
          reads=["ps1"], writes=["scrB"])
    for dc in range(DC):
        S.run("DVE", lambda h, dc=dc: h.tensor_tensor(
            C.xn[:, 512 * dc:512 * dc + 512], C.xtile[:, 512 * dc:512 * dc + 512], C.scrA, ALU.subtract),
            reads=[f"xt_{dc}", "scrA"], writes=[f"xn_{dc}"])
        S.run("DVE", lambda h, dc=dc: h.tensor_tensor(
            C.xn[:, 512 * dc:512 * dc + 512], C.xn[:, 512 * dc:512 * dc + 512], C.scrB, ALU.mult),
            reads=[f"xn_{dc}", "scrB"], writes=[f"xn_{dc}"])


def _wcol_load(S, C, mat, oc, ws, rkeys=()):
    dst = C.pw[ws][:].rearrange("p (dc c) -> p dc c", dc=DC, c=128)
    src = C.ap[mat][:, 128 * oc:128 * oc + 128].rearrange("(dc p) c -> p dc c", dc=DC, p=128)
    return S.dma("SP", lambda h: h.dma_start(out=dst, in_=src),
                 reads=list(rkeys), writes=[f"pw{ws}"], slot=f"pw{ws}")


def _phase_a(nc, S, C):
    if BUILD_STAGE < 1:
        return
    for tt in range(NSEG):
        _load_xtile(S, C, tt)
        _ln_tile(S, C)
        for hc in range(HC):
            S.dma("SP", lambda h, hc=hc: h.dma_start(
                out=C.pw[0][:].rearrange("p (dc c) -> p dc c", dc=DC, c=128),
                in_=C.ap["wgT0_f"][:, 128 * hc:128 * hc + 128].rearrange("(dc p) c -> p dc c", dc=DC, p=128)),
                reads=[f"wgT0_f_c{i}" for i in range(DC)], writes=["pw0"], slot="pw0")
            _acc(S, C.ps[6][:, :],
                 [(C.pw[0][:, 128 * dc:128 * dc + 128], C.xn[:, 512 * dc:512 * dc + 512]) for dc in range(DC)],
                 reads=["pw0"] + XN, writes=["ps6"])
            S.run("ACT", lambda h: h.activation(out=C.scrC, in_=C.ps[6][:, :], func=AF.Silu, scale=C127),
                  reads=["ps6"], writes=["scrC"])
            S.dma("SP", lambda h, hc=hc: h.dma_start(
                out=C.pw[1][:].rearrange("p (dc c) -> p dc c", dc=DC, c=128),
                in_=C.ap["wuT0_f"][:, 128 * hc:128 * hc + 128].rearrange("(dc p) c -> p dc c", dc=DC, p=128)),
                reads=[f"wuT0_f_c{i}" for i in range(DC)], writes=["pw1"], slot="pw1")
            _acc(S, C.ps[7][:, :],
                 [(C.pw[1][:, 128 * dc:128 * dc + 128], C.xn[:, 512 * dc:512 * dc + 512]) for dc in range(DC)],
                 reads=["pw1"] + XN, writes=["ps7"])
            hsl = C.hdb[:, 512 * (hc % 2):512 * (hc % 2) + 512]
            S.run("DVE", lambda h, hsl=hsl: h.scalar_tensor_tensor(
                hsl, C.ps[7][:, :], C127, C.scrC, ALU.mult, ALU.mult),
                reads=["ps7", "scrC"], writes=[f"hact{hc % 2}"])
            S.dma("SP", lambda h, hc=hc: h.dma_start(
                out=C.fw[hc % 2][:, 0:D], in_=C.ap["wdT0_f"][128 * hc:128 * hc + 128, :]),
                reads=[f"wdT0_f_c{hc}", f"fw{hc % 2}"], writes=[f"fw{hc % 2}"], slot=f"fwd{hc % 2}")
            def mm_down(h, hc=hc, hsl=hsl):
                last = None
                for do in range(6):
                    last = h.matmul(C.ps[do][:, :], C.fw[hc % 2][:, 128 * do:128 * do + 128], hsl,
                                    start=(hc == 0), stop=(hc == HC - 1))
                return last
            S.run("PE", mm_down, reads=[f"fw{hc % 2}", f"hact{hc % 2}"] + [f"ps{i}" for i in range(6)],
                  writes=[f"ps{i}" for i in range(6)])
            for j, do in enumerate((6, 7)):
                pb = C.ps[6 + j]
                S.run("PE", lambda h, do=do, pb=pb, hc=hc, hsl=hsl: h.matmul(
                    pb[:, :], C.fw[hc % 2][:, 128 * do:128 * do + 128], hsl, start=True, stop=True),
                    reads=[f"fw{hc % 2}", f"hact{hc % 2}", f"ps{6 + j}"], writes=[f"ps{6 + j}"])
                ysl = C.ysb[:, 512 * j:512 * j + 512]
                if hc == 0:
                    S.run("DVE", lambda h, ysl=ysl, pb=pb: h.tensor_copy(ysl, pb[:, :]),
                          reads=[f"ps{6 + j}"], writes=[f"ysb{j}"])
                else:
                    S.run("DVE", lambda h, ysl=ysl, pb=pb: h.tensor_tensor(ysl, ysl, pb[:, :], ALU.add),
                          reads=[f"ps{6 + j}", f"ysb{j}"], writes=[f"ysb{j}"])
        for do in range(6):
            S.run("ACT", lambda h, do=do: h.activation(
                out=C.stg[:, 512 * do:512 * do + 512], in_=C.ps[do][:, :], func=AF.Silu, scale=C127),
                reads=[f"ps{do}"], writes=[f"stg_{do}"])
        for j in range(2):
            S.run("ACT", lambda h, j=j: h.activation(
                out=C.stg[:, 512 * (6 + j):512 * (6 + j) + 512], in_=C.ysb[:, 512 * j:512 * j + 512],
                func=AF.Silu, scale=C127),
                reads=[f"ysb{j}"], writes=[f"stg_{6 + j}"])
        for do in range(DC):
            S.dma("POOL", lambda h, do=do, tt=tt: h.dma_start(
                out=C.ap["g_sp"][128 * do:128 * do + 128, 512 * tt:512 * tt + 512],
                in_=C.stg[:, 512 * do:512 * do + 512]),
                reads=[f"stg_{do}"], writes=[f"gsp_{do}_{tt}"], slot="gsp")

    if BUILD_STAGE < 2:
        return
    for sg in range(NSEG):
        _load_xtile(S, C, sg)
        _ln_tile(S, C)

        for mat, dst, nm in (("Wq0", C.qg, "qg"), ("Wk0", C.kg, "kg"), ("Wv0", C.vg, "vg")):
            for oc in range(DC):
                ws = oc % 2
                _wcol_load(S, C, mat, oc, ws)
                _acc(S, C.ps[oc % 2][:, :],
                     [(C.pw[ws][:, 128 * dc:128 * dc + 128], C.xn[:, 512 * dc:512 * dc + 512]) for dc in range(DC)],
                     reads=[f"pw{ws}"] + XN, writes=[f"ps{oc % 2}"])
                S.run("ACT", lambda h, oc=oc: h.activation(
                    out=C.stg[:, 512 * oc:512 * oc + 512], in_=C.ps[oc % 2][:, :], func=AF.Copy),
                    reads=[f"ps{oc % 2}"], writes=[f"stg_{oc}"])
            def gat(h, dst=dst):
                last = None
                for gp in range(2):
                    src = C.stg[64 * gp:64 * gp + 64, :].rearrange(
                        "p (g2 hh rr) -> p hh g2 rr", g2=8, hh=16, rr=32)
                    dd = dst[64 * gp:64 * gp + 64, :].rearrange(
                        "p (hh g2 rr) -> p hh g2 rr", hh=16, g2=8, rr=32)
                    last = h.tensor_copy(dd, src)
                return last
            S.run("DVE", gat, reads=STG, writes=[nm])

        for c8 in range(8):
            S.run("PE", lambda h, c8=c8: h.matmul(
                C.ps[2 + (c8 % 2)][:, :], C.SWAP, C.qg[:, 512 * c8:512 * c8 + 512], start=True, stop=True),
                reads=["qg", "swapm", f"ps{2 + (c8 % 2)}"], writes=[f"ps{2 + (c8 % 2)}"])
            S.run("ACT", lambda h, c8=c8: h.activation(
                out=C.qsw[:, 512 * c8:512 * c8 + 512], in_=C.ps[2 + (c8 % 2)][:, :], func=AF.Copy),
                reads=[f"ps{2 + (c8 % 2)}"], writes=["qsw"])

        if DEBUG:
            _dbg(S, C, f"qg_{sg}", C.qg[:, :], ["qg"])
            _dbg(S, C, f"kg_{sg}", C.kg[:, :], ["kg"])
            _dbg(S, C, f"vg_{sg}", C.vg[:, :], ["vg"])
            _dbg(S, C, f"xn_{sg}", C.xn[:, :], XN)

        if ATT_SUB >= 1:
            for head in range(H):
                _l0_head(nc, S, C, sg, head)

        if DEBUG:
            _dbg(S, C, f"sfm_{sg}", C.stg[:, :], [f"attw_{hh}" for hh in range(H)])

        if ATT_SUB < 7:
            continue
        attr = [f"attw_{hh}" for hh in range(H)]
        for oc in range(DC):
            ws = oc % 2
            _wcol_load(S, C, "Wo0", oc, ws)
            _acc(S, C.ps[oc % 2][:, :],
                 [(C.pw[ws][:, 128 * dc:128 * dc + 128], C.stg[:, 512 * dc:512 * dc + 512]) for dc in range(DC)],
                 reads=[f"pw{ws}"] + attr + STG, writes=[f"ps{oc % 2}"])
            S.dma("SP", lambda h, oc=oc, sg=sg: h.dma_start(
                out=C.scrC, in_=C.ap["g_sp"][128 * oc:128 * oc + 128, 512 * sg:512 * sg + 512]),
                reads=[f"gsp_{oc}_{sg}", "scrC"], writes=["scrC"], slot="gbk")
            S.run("DVE", lambda h, oc=oc: h.tensor_tensor(
                C.scrD, C.ps[oc % 2][:, :], C.scrC, ALU.mult),
                reads=[f"ps{oc % 2}", "scrC", "scrD"], writes=["scrD"])
            S.run("DVE", lambda h, oc=oc: h.tensor_tensor(
                C.scrD, C.scrD, C.xtile[:, 512 * oc:512 * oc + 512], ALU.add),
                reads=["scrD", f"xt_{oc}"], writes=["scrD"])
            S.dma("POOL", lambda h, oc=oc, sg=sg: h.dma_start(
                out=C.ap["x1_sp"][128 * oc:128 * oc + 128, 512 * sg:512 * sg + 512], in_=C.scrD),
                reads=["scrD"], writes=[f"x1sp_{oc}_{sg}"], slot="x1o")


def _l0_head(nc, S, C, sg, head):
    sub = ATT_SUB
    hof = 256 * head
    qh = C.qg[:, hof:hof + 256]
    qsh = C.qsw[:, hof:hof + 256]
    kh = C.kg[:, hof:hof + 256]
    vh = C.vg[:, hof:hof + 256]
    ONES = C.ONES
    IDENT = C.IDENT

    S.run("ACT", lambda h: h.activation(out=C.scrA[:, 0:256], in_=qh, func=AF.Square),
          reads=["qg", "scrA"], writes=["scrA"])
    def mm_qq(h):
        h.matmul(C.ps[4][0:1, 0:256], ONES[0:64, 0:1], C.scrA[0:64, 0:256], start=True, stop=True)
        return h.matmul(C.ps[5][0:1, 0:256], ONES[64:128, 64:65], C.scrA[64:128, 0:256],
                        start=True, stop=True)
    S.run("PE", mm_qq, reads=["scrA", "ones", "ps4", "ps5"], writes=["ps4", "ps5"])
    S.run("DVE", lambda h: h.reduce_max(C.srow[0:1, 0:1], C.ps[4][0:1, 0:256], axis=AX.X),
          reads=["ps4"], writes=["srow_qq"])
    S.run("DVE", lambda h: h.reduce_max(C.srow[0:1, 6:7], C.ps[5][0:1, 0:256], axis=AX.X),
          reads=["ps5"], writes=["srow_qq2"])
    S.run("ACT", lambda h: h.activation(out=C.scrA[:, 0:256], in_=kh, func=AF.Square),
          reads=["kg", "scrA", "ps4", "ps5"], writes=["scrA"])
    def mm_kk(h):
        h.matmul(C.ps[4][0:1, 0:256], ONES[0:64, 0:1], C.scrA[0:64, 0:256], start=True, stop=True)
        return h.matmul(C.ps[5][0:1, 0:256], ONES[64:128, 64:65], C.scrA[64:128, 0:256],
                        start=True, stop=True)
    S.run("PE", mm_kk, reads=["scrA", "ones", "ps4", "ps5", "srow_qq", "srow_qq2"], writes=["ps4", "ps5"])
    S.run("DVE", lambda h: h.reduce_max(C.srow[0:1, 1:2], C.ps[4][0:1, 0:256], axis=AX.X),
          reads=["ps4"], writes=["srow_kk"])
    S.run("DVE", lambda h: h.reduce_max(C.srow[0:1, 7:8], C.ps[5][0:1, 0:256], axis=AX.X),
          reads=["ps5"], writes=["srow_kk2"])
    S.run("DVE", lambda h: h.tensor_tensor(C.srow[0:1, 0:1], C.srow[0:1, 0:1], C.srow[0:1, 6:7], ALU.max),
          reads=["srow_qq", "srow_qq2"], writes=["srow_qq"])
    S.run("DVE", lambda h: h.tensor_tensor(C.srow[0:1, 1:2], C.srow[0:1, 1:2], C.srow[0:1, 7:8], ALU.max),
          reads=["srow_kk", "srow_kk2"], writes=["srow_kk"])
    S.run("DVE", lambda h: h.tensor_tensor(C.srow[0:1, 2:3], C.srow[0:1, 0:1], C.srow[0:1, 1:2], ALU.mult),
          reads=["srow_qq", "srow_kk"], writes=["srow_c"])
    S.run("ACT", lambda h: h.activation(out=C.srow[0:1, 3:4], in_=C.srow[0:1, 2:3], func=AF.Sqrt,
                                        scale=float(0.125 * 0.125)),
          reads=["srow_c"], writes=["srow_c"])
    S.run("DVE", lambda h: h.tensor_scalar(C.srow[0:1, 4:5], C.srow[0:1, 3:4], -1.0, None, ALU.mult, ALU.bypass),
          reads=["srow_c"], writes=["srow_cn"])
    S.run("PE", lambda h: h.matmul(C.ps[4][:, 0:1], ONES[0:1, 0:128], C.srow[0:1, 4:5], start=True, stop=True),
          reads=["srow_cn", "ones", "ps4", "srow_kk"], writes=["ps4"])
    S.run("ACT", lambda h: h.activation(out=C.cb[:, 0:1], in_=C.ps[4][:, 0:1], func=AF.Copy),
          reads=["ps4"], writes=["cb"])

    if sub < 2:
        return
    for jc in range(4):
        gpj, blk = jc // 2, jc % 2
        kslice = kh[64 * gpj:64 * gpj + 64, 128 * blk:128 * blk + 128]
        def mm_st(h, kslice=kslice, gpj=gpj):
            q_same = (qh if gpj == 0 else qsh)[64 * gpj:64 * gpj + 64, :]
            q_other = (qsh if gpj == 0 else qh)[64 * gpj:64 * gpj + 64, :]
            h.matmul(C.ps[2][:, 0:256], kslice, q_same, start=True, stop=True)
            return h.matmul(C.ps[3][:, 0:256], kslice, q_other, start=True, stop=True)
        S.run("PE", mm_st, reads=["qg", "qsw", "kg", "ps2", "ps3"], writes=["ps2", "ps3"])
        S.run("ACT", lambda h, jc=jc: h.activation(
            out=C.upool[:, 512 * jc:512 * jc + 256], in_=C.ps[2][:, 0:256], func=AF.Exp,
            bias=C.cb[:, 0:1], scale=0.125),
            reads=["ps2", "cb"], writes=[f"U{jc}"])
        S.run("ACT", lambda h, jc=jc: h.activation(
            out=C.upool[:, 512 * jc + 256:512 * jc + 512], in_=C.ps[3][:, 0:256], func=AF.Exp,
            bias=C.cb[:, 0:1], scale=0.125),
            reads=["ps3", "cb"], writes=[f"U{jc}"])
        S.run("DVE", lambda h, jc=jc: h.tensor_tensor(
            C.upool[:, 512 * jc:512 * jc + 512], C.upool[:, 512 * jc:512 * jc + 512],
            C.mask[:, 512 * jc:512 * jc + 512], ALU.mult),
            reads=[f"U{jc}", "mask"], writes=[f"U{jc}"])
    Ur = [f"U{jc}" for jc in range(4)]

    if sub < 3:
        return
    for jc in range(4):
        gpj, blk = jc // 2, jc % 2
        S.run("PE", lambda h, gpj=gpj, blk=blk: h.transpose(
            C.ps[4][0:128, 0:64], vh[64 * gpj:64 * gpj + 64, 128 * blk:128 * blk + 128],
            IDENT[64 * gpj:64 * gpj + 64, 64 * gpj:64 * gpj + 64]),
            reads=["vg", "ident", "ps4", "cb"], writes=["ps4"])
        S.run("ACT", lambda h, jc=jc: h.activation(out=C.tokp[:, 128 * jc:128 * jc + 64],
                                                   in_=C.ps[4][0:128, 0:64], func=AF.Copy),
              reads=["ps4"], writes=[f"vtok{jc}"])
    S.run("DVE", lambda h: h.tensor_scalar(C.scrA[:, 0:256], kh, 0.0, None, ALU.min, ALU.bypass),
          reads=["kg", "scrA"], writes=["scrA"])
    S.run("ACT", lambda h: h.activation(out=C.scrA[:, 0:256], in_=C.scrA[:, 0:256], func=AF.Exp),
          reads=["scrA"], writes=["scrA"])
    S.run("DVE", lambda h: h.tensor_scalar(kh, kh, 0.0, None, ALU.max, ALU.bypass),
          reads=["kg"], writes=["kg"])
    S.run("DVE", lambda h: h.tensor_tensor(kh, kh, C.scrA[:, 0:256], ALU.add),
          reads=["kg", "scrA"], writes=["kg", "sk_h"])
    for jc in range(4):
        gpj, blk = jc // 2, jc % 2
        S.run("PE", lambda h, gpj=gpj, blk=blk: h.transpose(
            C.ps[4][0:128, 0:64], kh[64 * gpj:64 * gpj + 64, 128 * blk:128 * blk + 128],
            IDENT[64 * gpj:64 * gpj + 64, 64 * gpj:64 * gpj + 64]),
            reads=["sk_h", "ident", "ps4"], writes=["ps4"])
        S.run("ACT", lambda h, jc=jc: h.activation(out=C.tokp[:, 512 + 128 * jc:512 + 128 * jc + 64],
                                                   in_=C.ps[4][0:128, 0:64], func=AF.Copy),
              reads=["ps4"], writes=[f"sktok{jc}"])
    S.run("DVE", lambda h: h.tensor_scalar(C.scrA[:, 0:256], qh, 0.0, None, ALU.min, ALU.bypass),
          reads=["qg", "scrA"], writes=["scrA"])
    S.run("ACT", lambda h: h.activation(out=C.scrA[:, 0:256], in_=C.scrA[:, 0:256], func=AF.Exp),
          reads=["scrA"], writes=["scrA"])
    S.run("DVE", lambda h: h.tensor_scalar(qh, qh, 0.0, None, ALU.max, ALU.bypass),
          reads=["qg"], writes=["qg"])
    S.run("DVE", lambda h: h.tensor_tensor(qh, qh, C.scrA[:, 0:256], ALU.add),
          reads=["qg", "scrA"], writes=["qg", "sq_h"])

    if sub < 4:
        return
    _acc(S, C.ps[5][0:1, :],
         [(ONES[:, 0:1], C.upool[:, 512 * jc:512 * jc + 512]) for jc in range(4)],
         reads=Ur + ["ones", "ps5"], writes=["ps5"])
    _acc(S, C.ps[6][0:64, :],
         [(C.tokp[:, 128 * jc:128 * jc + 64], C.upool[:, 512 * jc:512 * jc + 512]) for jc in range(4)],
         reads=Ur + [f"vtok{jc}" for jc in range(4)] + ["ps6"], writes=["ps6"])
    def mm_num(h):
        h.matmul(C.ps[7][0:64, 0:256], C.Md[0:64, 64 * head:64 * head + 64], qh[0:64, :],
                 start=True, stop=True)
        return h.matmul(C.ps[3][0:64, 0:256], C.Md[64:128, 64 * head:64 * head + 64], qh[64:128, :],
                        start=True, stop=True)
    S.run("PE", mm_num, reads=["sq_h", "Md", "ps7", "ps3"], writes=["ps7", "ps3"])
    S.run("DVE", lambda h: h.reciprocal(C.row[0:1, 1024:1536], C.ps[5][0:1, :]),
          reads=["ps5"], writes=["rowC"])
    def mm_den(h):
        h.matmul(C.ps[5][0:1, 0:256], C.Zd[0:64, head:head + 1], qh[0:64, :], start=True, stop=True)
        return h.matmul(C.ps[4][0:1, 0:256], C.Zd[64:128, head:head + 1], qh[64:128, :],
                        start=True, stop=True)
    S.run("PE", mm_den, reads=["sq_h", "Zd", "ps5", "ps4"], writes=["ps5", "ps4"])
    S.run("DVE", lambda h: h.reciprocal(C.row[0:1, 1536:1792], C.ps[5][0:1, 0:256]),
          reads=["ps5"], writes=["rowD"])
    S.run("DVE", lambda h: h.reciprocal(C.row[0:1, 1792:2048], C.ps[4][0:1, 0:256]),
          reads=["ps4"], writes=["rowD"])
    S.run("PE", lambda h: h.matmul(C.ps[2][0:64, :], ONES[0:1, 0:64], C.row[0:1, 1024:1536],
                                   start=True, stop=True),
          reads=["rowC", "ones", "ps2"], writes=["ps2"])
    S.run("PE", lambda h: h.matmul(C.ps[5][0:64, :], ONES[0:1, 0:64], C.row[0:1, 1536:2048],
                                   start=True, stop=True),
          reads=["rowD", "ones", "ps5"], writes=["ps5"])
    S.run("ACT", lambda h: h.activation(out=C.scrB[0:64, :], in_=C.ps[2][0:64, :], func=AF.Copy),
          reads=["ps2", "scrB"], writes=["scrB"])
    S.run("ACT", lambda h: h.activation(out=C.scrC[0:64, :], in_=C.ps[5][0:64, :], func=AF.Copy),
          reads=["ps5", "scrC"], writes=["scrC"])
    if sub < 5:
        return
    gcol = C.gates[:, H + head:H + head + 1]
    g1col = C.gates[:, 2 * H + head:2 * H + head + 1]
    S.run("DVE", lambda h: h.tensor_tensor(C.scrD[0:64, 0:256], C.ps[7][0:64, 0:256],
                                           C.scrC[0:64, 0:256], ALU.mult),
          reads=["ps7", "scrC", "scrD"], writes=["scrD"])
    S.run("DVE", lambda h: h.tensor_tensor(C.scrD[0:64, 256:512], C.ps[3][0:64, 0:256],
                                           C.scrC[0:64, 256:512], ALU.mult),
          reads=["ps3", "scrC", "scrD"], writes=["scrD"])
    S.run("DVE", lambda h: h.tensor_tensor(C.scrA[0:64, 0:512], C.ps[6][0:64, :], C.scrB[0:64, :], ALU.mult),
          reads=["ps6", "scrB", "scrA"], writes=["scrA"])
    S.run("DVE", lambda h: h.tensor_scalar(C.scrA[0:64, 0:512], C.scrA[0:64, 0:512], g1col, None,
                                           ALU.mult, ALU.bypass),
          reads=["scrA", "gate1m"], writes=["scrA"])
    S.run("DVE", lambda h: h.scalar_tensor_tensor(C.scrD[0:64, :], C.scrD[0:64, :], gcol,
                                                  C.scrA[0:64, 0:512], ALU.mult, ALU.add),
          reads=["scrD", "scrA", "gate"], writes=["scrD"])
    S.run("PE", lambda h: h.matmul(C.ps[2][:, :], C.DUP64, C.scrD[0:64, :], start=True, stop=True),
          reads=["scrD", "dup64", "ps2"], writes=["ps2"])
    def scat(h):
        last = None
        for gp in range(2):
            src = C.ps[2][64 * gp:64 * gp + 64, 256 * gp:256 * gp + 256].rearrange(
                "p (g2 rr) -> p g2 rr", g2=8, rr=32)
            dd = C.stg[64 * gp:64 * gp + 64, :].rearrange(
                "p (g2 f) -> p g2 f", g2=8, f=512)[:, :, 32 * head:32 * head + 32]
            last = h.tensor_copy(dd, src)
        return last
    S.run("DVE", scat, reads=["ps2"] + (STG if head == 0 else []),
          writes=[f"attw_{head}"] + (STG if head == 0 else []))

    if sub < 6:
        return
    _acc(S, C.ps[3][0:64, 0:64],
         [(C.tokp[:, 512 + 128 * jc:512 + 128 * jc + 64], C.tokp[:, 128 * jc:128 * jc + 64]) for jc in range(4)],
         reads=[f"sktok{jc}" for jc in range(4)] + [f"vtok{jc}" for jc in range(4)] + ["ps3"],
         writes=["ps3"])
    S.run("ACT", lambda h: h.activation(out=C.scrA[0:64, 0:64], in_=C.ps[3][0:64, 0:64], func=AF.Copy),
          reads=["ps3", "scrA"], writes=["scrA"])
    S.run("PE", lambda h: h.matmul(C.ps[3][:, 0:64], C.DUP64, C.scrA[0:64, 0:64], start=True, stop=True),
          reads=["scrA", "dup64", "ps3"], writes=["ps3"])
    S.run("DVE", lambda h: h.tensor_tensor(C.Md[:, 64 * head:64 * head + 64],
                                           C.Md[:, 64 * head:64 * head + 64],
                                           C.ps[3][:, 0:64], ALU.add),
          reads=["ps3", "Md", "ps7"], writes=["Md"])
    S.run("DVE", lambda h: h.tensor_reduce(C.srow[:, 5:6], kh, axis=AX.X, op=ALU.add),
          reads=["sk_h"], writes=["srow_dz"])
    S.run("PE", lambda h: h.matmul(C.ps[3][:, 64:65], C.DUPSW, C.srow[:, 5:6], start=True, stop=True),
          reads=["srow_dz", "dupsw", "ps3"], writes=["ps3"])
    S.run("DVE", lambda h: h.tensor_tensor(C.Zd[:, head:head + 1], C.Zd[:, head:head + 1],
                                           C.ps[3][:, 64:65], ALU.add),
          reads=["ps3", "Zd"], writes=["Zd"])


_nc_cache = None


def kernel(**inputs):
    global _nc_cache
    if _nc_cache is None:
        _nc_cache = build_program()
    in_maps = _prep_inputs(inputs)
    res = run_bass_kernel_spmd(_nc_cache, in_maps, core_ids=list(range(NCORES)))
    out = np.stack([np.ascontiguousarray(np.asarray(r["out_fm"]).T) for r in res.results])
    return out.astype(np.float32)
